# version 04
# baseline (speedup 1.0000x reference)
import contextlib
import numpy as np
import concourse.bass as bass
import concourse.mybir as mybir
from concourse.alu_op_type import AluOpType
from concourse.bass_utils import run_bass_kernel_spmd

F32 = mybir.dt.float32
BF16 = mybir.dt.bfloat16
I32 = mybir.dt.int32
AF = mybir.ActivationFunctionType
AX = mybir.AxisListType
OP = AluOpType

D = 2048
FC = 16
KV = 4
H = 16
HD = 128
F = 768
FFH = 8192
HC = 64
NTOK = 1024
NTC = 8
EPS_LN = 1e-5
EPS_A = 1e-8
EPS_L2 = 1e-12
EPS_RMS = 1e-8
DROWS = H * F
DEBUG_L1 = False
RACE_DETECT = False

SEM_NAMES = ["sPE", "sDV", "sAC", "sTr",
             "sXt0", "sXt1", "sMem0", "sMem1", "sWn0", "sWn1", "sBias",
             "sNrm", "sDsb0", "sDsb1", "sGd", "sWd", "sPo0", "sPo1",
             "sPin0", "sPin1", "sPin2", "sPin3", "sOut", "sDbg"]


class Ctr:
    def __init__(self, h):
        self.h = h
        self.v = 0

    def inc(self, instr, n=1):
        instr.then_inc(self.h, n)
        self.v += n
        return (self, self.v)

    def tok(self):
        return (self, self.v)


def wt(e, tok):
    if tok is not None:
        ctr, v = tok
        if v > 0:
            e.wait_ge(ctr.h, v)


COMPUTE_OPS = {"tensor_scalar", "tensor_tensor", "tensor_copy",
               "tensor_reduce", "scalar_tensor_tensor", "reciprocal",
               "memset", "activation", "select", "copy", "tensor_add",
               "tensor_mul", "tensor_sub", "tensor_max", "reduce_sum",
               "reduce_max"}


class G:

    def __init__(self, e, ctr):
        self._e = e
        self._c = ctr

    def __getattr__(self, name):
        f = getattr(self._e, name)
        if name in COMPUTE_OPS:
            c = self._c

            def wrapped(*a, **k):
                if c.v:
                    self._e.wait_ge(c.h, c.v)
                return c.inc(f(*a, **k))
            return wrapped
        return f


class B:
    def __init__(self, nc, es):
        self.nc = nc
        self.block = None
        self.es = es
        self.t = {}
        self.s = {}
        for n in SEM_NAMES:
            self.s[n] = Ctr(es.enter_context(nc.semaphore(n)))
        pp = [es.enter_context(nc.psum_tensor(f"pp{i}", [128, 1024], F32))
              for i in range(4)]
        self.pp = pp
        self.bank = [pp[i // 2][:, (i % 2) * 512:(i % 2) * 512 + 512]
                     for i in range(8)]
        self.bank_free = [None] * 8

    def sec(self, name, fn):
        if name == "vector":
            getattr(self.block, name)(lambda e: fn(G(e, self.s["sDV"])))
        elif name == "scalar":
            getattr(self.block, name)(lambda e: fn(G(e, self.s["sAC"])))
        else:
            getattr(self.block, name)(fn)

    def sb(self, name, shape, dt=F32):
        h = self.es.enter_context(self.nc.sbuf_tensor(name, shape, dt))
        self.t[name] = h
        return h



def tr_start(b, e, out, in_):
    c = b.s["sTr"]
    if c.v:
        e.wait_ge(c.h, c.v)
    return c.inc(e.dma_start_transpose(out=out, in_=in_), 16)


def emit_consts(b):
    def v(e):
        e.memset(b.t["ones"][:], 1.0)
        e.memset(b.t["epsln"][:], EPS_LN)
        e.memset(b.t["epsrms"][:], EPS_RMS)
        b.t["consts_done"] = b.s["sDV"].tok()
    b.sec("vector", v)


def emit_ln(b, x_dram, tc):
    s, t = b.s, b.t
    sel = tc % 2
    sX = s[f"sXt{sel}"]

    def ld(e):
        wt(e, t["xt_free"][sel])
        t["xt_ready"][sel] = sX.inc(
            e.dma_start(out=t["xt"][:, sel, :],
                        in_=x_dram[tc * 128:(tc + 1) * 128, :]), 16)
    b.sec("scalar", ld)

    def ve(e):
        wt(e, t["xt_ready"][sel])
        wt(e, t.get("tbuf_free"))
        xt = t["xt"][:, sel, :]
        e.tensor_scalar(out=t["junkv"][:], in0=xt, scalar1=1.0 / D,
                        scalar2=None, op0=OP.mult, op1=OP.add,
                        accum_out=t["mcol"][:])
        e.tensor_scalar(out=t["tbuf"][:], in0=xt, scalar1=t["mcol"][:],
                            scalar2=None, op0=OP.subtract)
        t["t_ready"] = s["sDV"].tok()
    b.sec("vector", ve)

    def ac(e):
        wt(e, t["t_ready"])
        e.activation(out=t["junkv"][:], in_=t["tbuf"][:], func=AF.Square,
                     accum_out=t["vcol"][:])
        e.activation(out=t["scol"][:], in_=t["vcol"][:], func=AF.Sqrt,
                         scale=1.0 / D, bias=t["epsln"][:])
        t["std_ready"] = s["sAC"].tok()
        t["tbuf_free"] = t["std_ready"]
    b.sec("scalar", ac)

    def ve2(e):
        wt(e, t["std_ready"])
        e.reciprocal(out=t["rcol"][:], in_=t["scol"][:])
        e.tensor_copy(out=t["stats_m"][:, tc:tc + 1], in_=t["mcol"][:])
        e.tensor_copy(out=t["stats_r"][:, tc:tc + 1], in_=t["rcol"][:])
        t["stats_ready"] = s["sDV"].tok()
    b.sec("vector", ve2)


def emit_nrm_load(b, nrm):
    s, t = b.s, b.t

    def sc0(e):
        t["nrm_ld"] = s["sNrm"].inc(
            e.dma_start(out=t["nrm_nat"][:],
                        in_=nrm.rearrange("h (a p) -> (h a) p", p=128)), 16)
    b.sec("scalar", sc0)

    def v0(e):
        wt(e, t["nrm_ld"])
        e.tensor_copy(out=t["nrm_natb"][:], in_=t["nrm_nat"][:])
        t["nrmb_ready"] = s["sDV"].tok()
    b.sec("vector", v0)

    def s0(e):
        wt(e, t["nrmb_ready"])
        t["nrmT_ready"] = tr_start(b, e, t["nrmT"][:], t["nrm_natb"][:])
    b.sec("sync", s0)


def emit_mem_aug_initial(b, mem, nrm):
    s, t = b.s, b.t
    emit_nrm_load(b, nrm)
    t["mtmp_free"] = [None, None]
    for i in range(96):
        hh, fcc = divmod(i, 6)
        sel = 0
        sM = s[f"sMem{i % 2}"]

        def sc(e, i=i, hh=hh, fcc=fcc, sel=sel, sM=sM):
            wt(e, t["mtmp_free"][sel])
            t["mtmp_ready"] = sM.inc(
                e.dma_start(out=t["mtmp"][:, sel, :],
                            in_=mem[hh, fcc * 128:(fcc + 1) * 128, :]), 16)
        b.sec("scalar", sc)

        def ve(e, i=i, sel=sel):
            wt(e, t["mtmp_ready"])
            if i == 0:
                wt(e, t["nrmT_ready"])
            e.tensor_copy(out=t["mem_aug"][:, i, 0:128],
                          in_=t["mtmp"][:, sel, :])
            e.tensor_copy(out=t["mem_aug"][:, i, 128:129],
                              in_=t["nrmT"][:, 0, i:i + 1])
            t["mtmp_free"][sel] = s["sDV"].tok()
        b.sec("vector", ve)
    t["memaug_done"] = s["sDV"].tok()


def emit_weight_cast_transpose(b, W, a, dst_slice, prev_consumer_tok,
                               wnat_name="wnat", nslots=2):
    s, t = b.s, b.t
    sel = t["wnat_rr"] = (t.get("wnat_rr", -1) + 1) % nslots
    sW = s[f"sWn{sel}"]

    def g(e, a=a, sel=sel, sW=sW, W=W):
        wt(e, t["wnat_free"][sel])
        t["wnat_ready"][sel] = sW.inc(
            e.dma_start(out=t[wnat_name][:, sel, :],
                        in_=W[a * 128:(a + 1) * 128, :]), 16)
    b.sec("gpsimd", g)

    def tr(e, sel=sel, dst_slice=dst_slice, prev=prev_consumer_tok):
        wt(e, t["wnat_ready"][sel])
        wt(e, prev)
        tok = tr_start(b, e, dst_slice, t[wnat_name][:, sel, :])
        t["wnat_free"][sel] = tok
        t["last_wtr"] = tok
    b.sec("sync", tr)



def emit_phase_a(b, x_dram, ntc, store_delta):
    s, t = b.s, b.t
    sPE, sDV, sAC, sTr = s["sPE"], s["sDV"], s["sAC"], s["sTr"]

    for tc in range(ntc):
        emit_ln(b, x_dram, tc)
        sel = tc % 2

        def ve3(e, tc=tc, sel=sel):
            wt(e, t["xnb_free"][sel])
            e.tensor_scalar(out=t["xnb"][:, sel, :], in0=t["tbuf"][:],
                                scalar1=t["rcol"][:], scalar2=None,
                                op0=OP.mult)
            t["xnb_ready"][sel] = sDV.tok()
            t["xt_free"][sel] = t["xnb_ready"][sel]
        b.sec("vector", ve3)

        def tr(e, tc=tc, sel=sel):
            wt(e, t["xnb_ready"][sel])
            t["xnb_free"][sel] = tr_start(b, e, t["xnfm"][:, tc, :, :],
                                          t["xnb"][:, sel, :])
        b.sec("sync", tr)
    xnfm_ready = sTr.tok()

    for kv in range(KV):
        for j in range(4):
            emit_weight_cast_transpose(
                b, t["Wmb_dram"], kv * 4 + j, t["WmbT"][:, j, :, :],
                t.get(("a2a_done", kv - 1)))
        wmb_slab_tr = t["last_wtr"]

        for tc in range(ntc):
            pk = b.bank[0 + tc % 2]
            pv = b.bank[2 + tc % 2]
            pm = b.bank[4 + tc % 2]

            def pe(e, kv=kv, tc=tc, pk=pk, pv=pv, pm=pm,
                   wmb_slab_tr=wmb_slab_tr):
                if kv == 0 and tc == 0:
                    wt(e, xnfm_ready)
                    for tok in t["wA_done"]:
                        wt(e, tok)
                if tc == 0:
                    wt(e, wmb_slab_tr)
                for k in (0 + tc % 2, 2 + tc % 2, 4 + tc % 2):
                    wt(e, b.bank_free[k])
                for fc in range(FC):
                    lhs = t["xnfm"][:, tc, fc, :]
                    e.matmul(out=pk[:, 0:128], lhsT=lhs,
                             rhs=t["WkT"][:, kv, fc, :],
                             start=(fc == 0), stop=False)
                    e.matmul(out=pv[:, 0:128], lhsT=lhs,
                             rhs=t["WvT"][:, kv, fc, :],
                             start=(fc == 0), stop=False)
                    e.matmul(out=pm[:, 0:512], lhsT=lhs,
                             rhs=t["WmbT"][:, :, fc, :],
                             start=(fc == 0), stop=False)
                e.matmul(out=pk[:, 0:128], lhsT=t["ones"][:, 0:128],
                         rhs=t["bkr"][0:1, kv * 128:(kv + 1) * 128],
                         start=False, stop=True)
                e.matmul(out=pv[:, 0:128], lhsT=t["ones"][:, 0:128],
                         rhs=t["bvr"][0:1, kv * 128:(kv + 1) * 128],
                         start=False, stop=True)
                t["proj_done"][tc % 2] = sPE.inc(
                    e.matmul(out=pm[:, 0:512], lhsT=t["ones"][:, 0:128],
                             rhs=t["bmbr"][0:1, kv * 512:(kv + 1) * 512],
                             start=False, stop=True))
            b.sec("tensor", pe)

            def vek(e, kv=kv, tc=tc, pk=pk, pv=pv):
                wt(e, t["proj_done"][tc % 2])
                e.tensor_copy(out=t["kf"][:, tc, :], in_=pk[:, 0:128])
                e.tensor_copy(out=t["vb"][:, tc, :],
                                            in_=pv[:, 0:128])
                tok = sDV.tok()
                b.bank_free[0 + tc % 2] = tok
                b.bank_free[2 + tc % 2] = tok
                t["kf_ready"][tc] = tok
            b.sec("vector", vek)

            def acm(e, kv=kv, tc=tc, pm=pm):
                wt(e, t["proj_done"][tc % 2])
                e.activation(out=t["mbb"][:, tc, :],
                                           in_=pm[:, 0:512], func=AF.Sigmoid)
                tok = sAC.tok()
                b.bank_free[4 + tc % 2] = tok
            b.sec("scalar", acm)
        t[("a2a_done", kv)] = sPE.tok()

        for tc in range(ntc):
            def acr(e, kv=kv, tc=tc):
                wt(e, t["kf_ready"][tc])
                wt(e, t.get("mkpre_ready"))
                kf = t["kf"][:, tc, :]
                e.activation(out=t["xp"][:, 0:128], in_=kf, func=AF.Relu)
                e.activation(out=t["xp"][:, 128:256], in_=kf,
                                 func=AF.Relu, scale=-1.0)
                t["xp_ready"] = sAC.tok()
            b.sec("scalar", acr)

            def ver(e, kv=kv, tc=tc):
                wt(e, t["xp_ready"])
                xp, mp = t["xp"], t["mk_pre"]
                for j in range(3):
                    sh = j + 1
                    e.tensor_tensor(out=mp[:, j * 256 + sh:j * 256 + 256],
                                    in0=xp[:, sh:256], in1=xp[:, 0:256 - sh],
                                    op=OP.mult)
                    e.tensor_tensor(out=mp[:, j * 256:j * 256 + sh],
                                    in0=xp[:, 0:sh],
                                    in1=xp[:, 256 - sh:256], op=OP.mult)
                t["mkpre_ready"] = sDV.tok()
            b.sec("vector", ver)

            def acs(e, kv=kv, tc=tc):
                wt(e, t["mkpre_ready"])
                e.activation(out=t["junka"][:, 0:F], in_=t["mk_pre"][:],
                             func=AF.Square, accum_out=t["ssq"][:])
                e.activation(out=t["ncol"][:], in_=t["ssq"][:],
                                 func=AF.Sqrt)
                t["n_ready"] = sAC.tok()
            b.sec("scalar", acs)

            def vemk(e, kv=kv, tc=tc):
                wt(e, t["n_ready"])
                wt(e, t.get("mk_guard"))
                e.tensor_scalar(out=t["nmax"][:], in0=t["ncol"][:],
                                scalar1=EPS_L2, scalar2=None, op0=OP.max)
                e.reciprocal(out=t["inv"][:], in_=t["nmax"][:])
                e.tensor_scalar(out=t["mk_tm"][:, tc, :], in0=t["mk_pre"][:],
                                scalar1=t["inv"][:], scalar2=None, op0=OP.mult)
                e.tensor_tensor(out=t["tmp1"][:], in0=t["ssq"][:],
                                in1=t["inv"][:], op=OP.mult)
                e.tensor_tensor(out=t["msq"][:], in0=t["tmp1"][:],
                                    in1=t["inv"][:], op=OP.mult)
                t["mk_ready"] = sDV.tok()
            b.sec("vector", vemk)

            def trmk(e, kv=kv, tc=tc):
                wt(e, t["mk_ready"])
                wt(e, t.get("mkfm_free"))
                t["mkfm_ready"] = tr_start(b, e, t["mkfm"][:],
                                           t["mk_tm"][:, tc, :])
            b.sec("sync", trmk)

            for hl in range(4):
                h = kv * 4 + hl
                pn = b.bank[6 + hl % 2]

                def pen(e, kv=kv, tc=tc, hl=hl, h=h, pn=pn):
                    if hl == 0:
                        wt(e, t["mkfm_ready"])
                    wt(e, b.bank_free[6 + hl % 2])
                    for fcc in range(6):
                        mm = e.matmul(out=pn[:, 0:129],
                                      lhsT=t["mkfm"][:, fcc, :],
                                      rhs=t["mem_aug"][:, h * 6 + fcc, :],
                                      start=(fcc == 0), stop=(fcc == 5))
                        if fcc == 5:
                            t["pn_done"] = sPE.inc(mm)
                    if hl == 3:
                        t["mkfm_free"] = t["pn_done"]
                b.sec("tensor", pen)

                def vew(e, kv=kv, tc=tc, hl=hl, h=h, pn=pn):
                    wt(e, t["pn_done"])
                    dcol, rden, nrd = t["dcol"], t["rden"], t["nrd"]
                    e.tensor_scalar(out=dcol[:], in0=pn[:, 128:129],
                                    scalar1=EPS_A, scalar2=None, op0=OP.add)
                    e.reciprocal(out=rden[:], in_=dcol[:])
                    e.tensor_scalar(out=nrd[:], in0=rden[:], scalar1=-1.0,
                                    scalar2=None, op0=OP.mult)
                    e.reciprocal(out=t["tmp1"][:], in_=t["msq"][:])
                    e.tensor_tensor(out=t["tmp2"][:], in0=dcol[:],
                                    in1=t["tmp1"][:], op=OP.mult)
                    e.tensor_scalar(out=t["tmp1"][:], in0=t["tmp2"][:],
                                    scalar1=-1.0, scalar2=1.0, op0=OP.mult,
                                    op1=OP.add)
                    e.tensor_scalar(out=t["wmv"][:, hl, tc, 128:129],
                                    in0=t["tmp1"][:], scalar1=0.0, scalar2=1.0,
                                    op0=OP.max, op1=OP.min)
                    e.scalar_tensor_tensor(out=t["mvb"][:], in0=pn[:, 0:128],
                                           scalar=nrd[:], in1=t["vb"][:, tc, :],
                                           op0=OP.mult, op1=OP.add)
                    e.tensor_tensor(
                        out=t["wmv"][:, hl, tc, 0:128], in0=t["mvb"][:],
                        in1=t["mbb"][:, tc, hl * 128:hl * 128 + 128],
                        op=OP.mult)
                    tok = sDV.tok()
                    b.bank_free[6 + hl % 2] = tok
                    t["wmv_ready"] = tok
                b.sec("vector", vew)

        for hl in range(4):
            h = kv * 4 + hl
            for fcc in range(6):
                alt = (hl * 6 + fcc) % 2
                pd = b.bank[0 + alt]

                def ped(e, kv=kv, hl=hl, h=h, fcc=fcc, alt=alt, pd=pd):
                    if hl == 0 and fcc == 0:
                        wt(e, t["wmv_ready"])
                    wt(e, b.bank_free[0 + alt])
                    for tc in range(ntc):
                        mm = e.matmul(
                            out=pd[:, 0:129],
                            lhsT=t["mk_tm"][:, tc, fcc * 128:fcc * 128 + 128],
                            rhs=t["wmv"][:, hl, tc, :],
                            start=(tc == 0), stop=(tc == ntc - 1))
                        if tc == ntc - 1:
                            t["pd_done"] = sPE.inc(mm)
                b.sec("tensor", ped)

                def ved(e, kv=kv, hl=hl, h=h, fcc=fcc, alt=alt, pd=pd):
                    wt(e, t["pd_done"])
                    wt(e, t["dsb_free"][alt])
                    e.tensor_copy(out=t["dsb"][:, alt, :],
                                                in_=pd[:, 0:129])
                    tok = sDV.tok()
                    b.bank_free[0 + alt] = tok
                    t["dsb_ready"][alt] = tok
                b.sec("vector", ved)

                store_delta(b, h, fcc, alt)
        t["mk_guard"] = sPE.tok()


def phase_a_alloc(b, ntc):
    b.sb("wnat", [128, 2, D], BF16)
    b.sb("WkT", [128, 4, FC, 128], BF16)
    b.sb("WvT", [128, 4, FC, 128], BF16)
    b.sb("WmbT", [128, 4, FC, 128], BF16)
    b.sb("bkr", [1, 512], BF16)
    b.sb("bvr", [1, 512], BF16)
    b.sb("bmbr", [1, D], BF16)
    b.sb("ones", [1, 128], BF16)
    b.sb("epsln", [128, 1], F32)
    b.sb("epsrms", [128, 1], F32)
    b.sb("mem_aug", [128, H * 6, 129], BF16)
    b.sb("mtmp", [128, 2, 128], F32)
    b.sb("nrm_nat", [96, 128], F32)
    b.sb("nrm_natb", [96, 128], BF16)
    b.sb("nrmT", [128, 1, 96], BF16)
    b.sb("xt", [128, 2, D], F32)
    b.sb("xnb", [128, 2, D], BF16)
    b.sb("xnfm", [128, ntc, FC, 128], BF16)
    b.sb("junkv", [128, D], BF16)
    b.sb("junka", [128, F], BF16)
    b.sb("tbuf", [128, D], F32)
    for n in ["mcol", "vcol", "scol", "rcol", "ssq", "ncol", "nmax", "inv",
              "msq", "tmp1", "tmp2", "dcol", "rden", "nrd"]:
        b.sb(n, [128, 1], F32)
    b.sb("stats_m", [128, ntc], F32)
    b.sb("stats_r", [128, ntc], F32)
    b.sb("kf", [128, ntc, 128], F32)
    b.sb("vb", [128, ntc, 128], BF16)
    b.sb("mbb", [128, ntc, 512], BF16)
    b.sb("xp", [128, 256], F32)
    b.sb("mk_pre", [128, F], F32)
    b.sb("mk_tm", [128, ntc, F], BF16)
    b.sb("mkfm", [128, 6, 128], BF16)
    b.sb("mvb", [128, 128], F32)
    b.sb("wmv", [128, 4, ntc, 129], BF16)
    b.sb("dsb", [128, 2, 129], F32)
    t = b.t
    t["xt_free"] = [None, None]
    t["xt_ready"] = [None, None]
    t["xnb_free"] = [None, None]
    t["xnb_ready"] = [None, None]
    t["proj_done"] = [None, None]
    t["kf_ready"] = [None] * ntc
    t["dsb_free"] = [None, None]
    t["dsb_ready"] = [None, None]
    t["wnat_free"] = [None, None]
    t["wnat_ready"] = [None, None]


def build_launch1():
    nc = bass.Bass("TRN2", target_bir_lowering=False,
                   detect_race_conditions=RACE_DETECT)
    x = nc.dram_tensor("x", [NTOK, D], F32, kind="ExternalInput").ap()
    Wk = nc.dram_tensor("Wk", [512, D], F32, kind="ExternalInput").ap()
    bk = nc.dram_tensor("bk", [512], F32, kind="ExternalInput").ap()
    Wv = nc.dram_tensor("Wv", [512, D], F32, kind="ExternalInput").ap()
    bv = nc.dram_tensor("bv", [512], F32, kind="ExternalInput").ap()
    Wmb = nc.dram_tensor("Wmb", [D, D], F32, kind="ExternalInput").ap()
    bmb = nc.dram_tensor("bmb", [D], F32, kind="ExternalInput").ap()
    mem = nc.dram_tensor("mem", [H, F, HD], F32, kind="ExternalInput").ap()
    nrm = nc.dram_tensor("nrm", [H, F], F32, kind="ExternalInput").ap()
    delta = nc.dram_tensor("delta", [DROWS, 129], F32,
                           kind="ExternalOutput").ap()

    es = contextlib.ExitStack()
    with es:
        b = B(nc, es)
        s, t = b.s, b.t
        phase_a_alloc(b, NTC)
        t["Wmb_dram"] = Wmb
        dbg = {}
        if DEBUG_L1:
            for nm, shape, dtp in [("xnfm", [128, NTC, FC, 128], BF16),
                                   ("kf", [128, NTC, 128], F32),
                                   ("vb", [128, NTC, 128], BF16),
                                   ("mbb", [128, NTC, 512], BF16),
                                   ("mk_tm", [128, NTC, F], BF16),
                                   ("wmv", [128, 4, NTC, 129], BF16),
                                   ("mem_aug", [128, H * 6, 129], BF16)]:
                dbg[nm] = nc.dram_tensor("dbg_" + nm, shape, dtp,
                                         kind="ExternalOutput").ap()
        with nc.Block() as block:
            b.block = block
            emit_consts(b)

            def gb(e):
                sB = s["sBias"]
                e.dma_start(out=t["bkr"][:],
                            in_=bk.rearrange("(o a) -> o a", o=1)).then_inc(
                    sB.h, 16)
                sB.v += 16
                e.dma_start(out=t["bvr"][:],
                            in_=bv.rearrange("(o a) -> o a", o=1)).then_inc(
                    sB.h, 16)
                sB.v += 16
                e.dma_start(out=t["bmbr"][:],
                            in_=bmb.rearrange("(o a) -> o a", o=1)).then_inc(
                    sB.h, 16)
                sB.v += 16
            b.sec("gpsimd", gb)
            bias_done = s["sBias"].tok()

            for a in range(4):
                emit_weight_cast_transpose(b, Wk, a, t["WkT"][:, a, :, :],
                                           None)
            for a in range(4):
                emit_weight_cast_transpose(b, Wv, a, t["WvT"][:, a, :, :],
                                           None)
            wT_done = t["last_wtr"]
            emit_mem_aug_initial(b, mem, nrm)

            t["wA_done"] = [wT_done, t["memaug_done"], bias_done,
                            t["consts_done"]]

            def store_delta(bb, h, fcc, alt):
                sD = s[f"sDsb{alt}"]

                def sc(e, h=h, fcc=fcc, alt=alt, sD=sD):
                    wt(e, t["dsb_ready"][alt])
                    row = h * F + fcc * 128
                    t["dsb_free"][alt] = sD.inc(
                        e.dma_start(out=delta[row:row + 128, :],
                                    in_=t["dsb"][:, alt, :]), 16)
                bb.sec("scalar", sc)

            emit_phase_a(b, x, NTC, store_delta)

            if DEBUG_L1:
                def dmp(e):
                    for c in b.s.values():
                        if c.v:
                            e.wait_ge(c.h, c.v)
                    for nm in dbg:
                        s["sDbg"].inc(e.dma_start(out=dbg[nm], in_=t[nm][:]),
                                      16)
                b.sec("scalar", dmp)

            def fin(e):
                e.wait_ge(s["sDsb0"].h, s["sDsb0"].v)
                e.wait_ge(s["sDsb1"].h, s["sDsb1"].v)
                if DEBUG_L1:
                    e.wait_ge(s["sDbg"].h, s["sDbg"].v)
            b.sec("sync", fin)
    return nc



def build_launch2():
    nc = bass.Bass("TRN2", target_bir_lowering=False,
                   detect_race_conditions=RACE_DETECT)
    x = nc.dram_tensor("x", [NTOK, D], F32, kind="ExternalInput").ap()
    Wq = nc.dram_tensor("Wq", [D, D], F32, kind="ExternalInput").ap()
    bq = nc.dram_tensor("bq", [D], F32, kind="ExternalInput").ap()
    Wg = nc.dram_tensor("Wg", [D, D], F32, kind="ExternalInput").ap()
    bg = nc.dram_tensor("bg", [D], F32, kind="ExternalInput").ap()
    mem = nc.dram_tensor("mem", [H, F, HD], F32, kind="ExternalInput").ap()
    nrm = nc.dram_tensor("nrm", [H, F], F32, kind="ExternalInput").ap()
    Wgf = nc.dram_tensor("Wgf", [FFH, D], F32, kind="ExternalInput").ap()
    Wuf = nc.dram_tensor("Wuf", [FFH, D], F32, kind="ExternalInput").ap()
    Wdf = nc.dram_tensor("Wdf", [D, FFH], F32, kind="ExternalInput").ap()
    dl = [nc.dram_tensor(f"d{i}", [DROWS, 129], F32, kind="ExternalInput").ap()
          for i in range(8)]
    out = nc.dram_tensor("out", [NTOK, D], F32, kind="ExternalOutput").ap()
    dsum = nc.dram_tensor("dsum", [DROWS, 129], F32, kind="Internal").ap()
    wd_sgn = nc.dram_tensor("wd_sgn", [D, FFH], BF16, kind="Internal").ap()
    p_dram = nc.dram_tensor("p_dram", [HC, 128, NTOK], BF16,
                            kind="Internal").ap()

    es = contextlib.ExitStack()
    esx = contextlib.ExitStack()
    es1 = contextlib.ExitStack()
    es2 = contextlib.ExitStack()
    with es:
        b = B(nc, es)
        s, t = b.s, b.t
        alloc_l2_global(b)
        b.es = esx
        b.sb("xqfm", [128, NTC, FC, 128], BF16)
        b.es = es1
        alloc_l2_s1(b)
        b.es = es
        with nc.Block() as block:
            b.block = block
            emit_consts(b)

            def gb(e):
                sB = s["sBias"]
                e.dma_start(out=t["bqr"][:],
                            in_=bq.rearrange("(o a) -> o a", o=1)).then_inc(
                    sB.h, 16)
                sB.v += 16
                e.dma_start(out=t["bgr"][:],
                            in_=bg.rearrange("(o a) -> o a", o=1)).then_inc(
                    sB.h, 16)
                sB.v += 16
            b.sec("gpsimd", gb)
            t["bias_done"] = s["sBias"].tok()

            emit_wd_prep(b, Wdf, wd_sgn)
            emit_f0_quant(b, x)
            emit_barrier(b)
            es1.close()
            b.es = es2
            alloc_l2_s2(b)
            b.es = es
            emit_ffn(b, Wgf, Wuf, wd_sgn, p_dram)

            def g_dsum(e):
                t["dsum_tok"] = s["sGd"].inc(e.dma_start(out=dsum, in_=dl[0]),
                                             16)
                for i in range(1, 8):
                    wt(e, t["dsum_tok"])
                    t["dsum_tok"] = s["sGd"].inc(
                        e.dma_start(out=dsum, in_=dl[i], accum_op=OP.add,
                                    max_dma_last_dim=8192), 16)
            b.sec("gpsimd", g_dsum)
            t["dsum_done"] = s["sGd"].tok()

            emit_barrier(b)
            es2.close()
            esx.close()
            alloc_l2_b(b)
            emit_new_mem_aug(b, mem, nrm, dsum)
            emit_phase_b(b, x, out, Wq, Wg)

            def fin(e):
                e.wait_ge(s["sOut"].h, s["sOut"].v)
            b.sec("sync", fin)
    return nc


def emit_barrier(b):
    vals = [(c.h, c.v) for c in b.s.values() if c.v > 0]

    for eng in ["tensor", "vector", "scalar", "sync", "gpsimd"]:
        def w(e, vals=vals):
            for h, v in vals:
                e.wait_ge(h, v)
        b.sec(eng, w)


def alloc_l2_global(b):
    b.sb("ones", [1, 128], BF16)
    b.sb("epsln", [128, 1], F32)
    b.sb("epsrms", [128, 1], F32)
    b.sb("bqr", [1, D], BF16)
    b.sb("bgr", [1, D], BF16)
    b.sb("mtmp", [128, 1, 128], F32)
    b.sb("dtile", [128, 1, 129], F32)
    b.sb("nrm_nat", [96, 128], F32)
    b.sb("nrm_natb", [96, 128], BF16)
    b.sb("nrmT", [128, 1, 96], BF16)
    b.sb("xt", [128, 2, D], F32)
    b.sb("junka", [128, F], BF16)
    for n in ["mcol", "vcol", "scol", "rcol", "m2col", "v2col", "s2col",
              "r2col", "amx", "rmx", "scq", "ssq", "ncol", "nmax", "inv",
              "dcol", "rden", "tmp1"]:
        b.sb(n, [128, 1], F32)
    b.sb("stats_m", [128, NTC], F32)
    b.sb("stats_r", [128, NTC], F32)
    b.sb("gf", [128, NTC, D], BF16)
    b.sb("xp", [128, 256], F32)
    b.sb("mq_pre", [128, F], F32)
    b.sb("mq_tm", [128, F], BF16)
    b.sb("mqfm", [128, 6, 128], BF16)
    b.sb("mhb", [128, 128], F32)
    b.sb("sh", [128, 128], F32)
    b.sb("oh", [128, 128], F32)
    b.sb("xnh", [128, 128], F32)
    b.sb("outsb", [128, D], F32)
    b.sb("xnbB", [128, D], BF16)
    t = b.t
    t["xt_free"] = [None, None]
    t["xt_ready"] = [None, None]
    t["wnat_free"] = [None, None]
    t["wnat_ready"] = [None, None]


def alloc_l2_s1(b):
    b.sb("junkv", [128, D], BF16)
    b.sb("tbuf", [128, D], F32)
    b.sb("t2buf", [128, D], F32)
    b.sb("ubuf", [128, D], F32)
    b.sb("ibuf", [128, D], I32)
    b.sb("cbuf", [128, D], F32)
    b.sb("rbuf", [128, D], F32)
    b.sb("crnd", [128, D], F32)
    b.sb("xqb", [128, D], BF16)
    b.sb("wdnat", [128, 1, FFH], BF16)


def alloc_l2_s2(b):
    b.sb("wnat", [128, 2, D], BF16)
    b.sb("wgT", [128, 2, FC, 128], BF16)
    b.sb("wuT", [128, 2, FC, 128], BF16)
    b.sb("sgb", [128, 1024], F32)
    b.sb("silu", [128, 1024], F32)
    b.sb("pout", [128, 2, NTOK], BF16)
    b.sb("pin", [128, 4, NTOK], BF16)
    b.sb("wdT", [128, 2, 1, NTOK], BF16)
    b.sb("sgb2", [128, 2, 512], F32)


def alloc_l2_b(b):
    b.sb("mem_aug", [128, H * 6, 129], BF16)
    b.sb("WqTs", [128, 4, FC, 128], BF16)
    b.sb("WgTs", [128, 4, FC, 128], BF16)
    b.sb("qf4", [128, 4, D], F32)
    b.sb("gate4", [128, 4, D], BF16)
    b.sb("xnfmB", [128, 4, FC, 128], BF16)
    b.sb("wnatB", [128, 1, D], BF16)


def emit_wd_prep(b, Wdf, wd_sgn):
    s, t = b.s, b.t
    t["wd_store"] = None
    for a in range(FC):
        def g(e, a=a):
            wt(e, t["wd_store"])
            t["wdnat_ready"] = s["sWd"].inc(
                e.dma_start(out=t["wdnat"][:, 0, :],
                            in_=Wdf[a * 128:(a + 1) * 128, :]), 16)
        b.sec("gpsimd", g)

        def asg(e, a=a):
            wt(e, t["wdnat_ready"])
            e.activation(out=t["wdnat"][:, 0, :], in_=t["wdnat"][:, 0, :],
                             func=AF.Sign)
            t["wdnat_sgn"] = s["sAC"].tok()
        b.sec("scalar", asg)

        def st(e, a=a):
            wt(e, t["wdnat_sgn"])
            t["wd_store"] = s["sWd"].inc(
                e.dma_start(out=wd_sgn[a * 128:(a + 1) * 128, :],
                            in_=t["wdnat"][:, 0, :]), 16)
        b.sec("scalar", st)
    t["wd_scratch_done"] = t["wd_store"]


def emit_f0_quant(b, x):
    s, t = b.s, b.t
    for tc in range(NTC):
        sel = tc % 2
        emit_ln(b, x, tc)

        def ve2(e, tc=tc, sel=sel):
            wt(e, t.get("c1_ready"))
            e.tensor_scalar(out=t["cbuf"][:], in0=t["tbuf"][:],
                            scalar1=t["rcol"][:], scalar2=None, op0=OP.mult)
            e.tensor_scalar(out=t["junkv"][:], in0=t["cbuf"][:],
                            scalar1=1.0 / D, scalar2=None, op0=OP.mult,
                            op1=OP.add, accum_out=t["m2col"][:])
            e.tensor_scalar(out=t["t2buf"][:], in0=t["cbuf"][:],
                            scalar1=t["m2col"][:], scalar2=None,
                            op0=OP.subtract)
            e.tensor_reduce(out=t["amx"][:], in_=t["t2buf"][:], axis=AX.X,
                            op=OP.max, apply_absolute_value=True)
            e.reciprocal(out=t["rmx"][:], in_=t["amx"][:])
            e.tensor_scalar(out=t["ubuf"][:], in0=t["t2buf"][:],
                                scalar1=t["rmx"][:], scalar2=127.0,
                                op0=OP.mult, op1=OP.mult)
            t["u_ready"] = s["sDV"].tok()
            t["xt_free"][sel] = t["u_ready"]
        b.sec("vector", ve2)

        def ac2(e, tc=tc):
            wt(e, t["u_ready"])
            e.activation(out=t["junkv"][:], in_=t["t2buf"][:], func=AF.Square,
                         accum_out=t["v2col"][:])
            e.activation(out=t["s2col"][:], in_=t["v2col"][:], func=AF.Sqrt,
                         scale=1.0 / D, bias=t["epsrms"][:])
            e.activation(out=t["ibuf"][:], in_=t["ubuf"][:], func=AF.Identity)
            e.activation(out=t["cbuf"][:], in_=t["ibuf"][:],
                             func=AF.Identity)
            t["c1_ready"] = s["sAC"].tok()
        b.sec("scalar", ac2)

        def ve3(e, tc=tc):
            wt(e, t["c1_ready"])
            wt(e, t.get("xqb_free"))
            e.reciprocal(out=t["r2col"][:], in_=t["s2col"][:])
            e.tensor_tensor(out=t["rbuf"][:], in0=t["ubuf"][:],
                            in1=t["cbuf"][:], op=OP.subtract)
            e.scalar_tensor_tensor(out=t["crnd"][:], in0=t["rbuf"][:],
                                   scalar=0.5, in1=t["cbuf"][:],
                                   op0=OP.is_gt, op1=OP.add)
            e.tensor_scalar(out=t["tbuf"][:], in0=t["rbuf"][:], scalar1=-0.5,
                            scalar2=-1.0, op0=OP.is_lt, op1=OP.mult)
            e.tensor_tensor(out=t["crnd"][:], in0=t["crnd"][:],
                            in1=t["tbuf"][:], op=OP.add)
            e.tensor_scalar(out=t["crnd"][:], in0=t["crnd"][:], scalar1=-128.0,
                            scalar2=127.0, op0=OP.max, op1=OP.min)
            e.tensor_tensor(out=t["scq"][:], in0=t["amx"][:], in1=t["r2col"][:],
                            op=OP.mult)
            e.tensor_scalar(out=t["scq"][:], in0=t["scq"][:],
                            scalar1=1.0 / 127.0, scalar2=None, op0=OP.mult)
            e.tensor_scalar(out=t["xqb"][:], in0=t["crnd"][:],
                                scalar1=t["scq"][:], scalar2=None,
                                op0=OP.mult)
            t["xqb_ready"] = s["sDV"].tok()
        b.sec("vector", ve3)

        def tr(e, tc=tc):
            wt(e, t["xqb_ready"])
            t["xqb_free"] = tr_start(b, e, t["xqfm"][:, tc, :, :],
                                     t["xqb"][:])
        b.sec("sync", tr)
    t["xq_done"] = s["sTr"].tok()


def emit_ffn(b, Wgf, Wuf, wd_sgn, p_dram):
    s, t = b.s, b.t
    sPE, sDV, sAC, sTr = s["sPE"], s["sDV"], s["sAC"], s["sTr"]

    t["pg_free"] = [None, None]
    t["pout_free"] = [None, None]
    for hc in range(HC):
        dsel = hc % 2
        for m, W in [(0, Wgf), (1, Wuf)]:
            dst = (t["wgT"] if m == 0 else t["wuT"])
            emit_weight_cast_transpose(b, W, hc, dst[:, dsel, :, :],
                                       t.get(("mm12_done", hc - 2)))
            wtr = t["last_wtr"]

            def sg(e, hc=hc, m=m, dsel=dsel, wtr=wtr):
                wt(e, wtr)
                dst2 = t["wgT"] if m == 0 else t["wuT"]
                e.activation(out=dst2[:, dsel, :, :],
                                 in_=dst2[:, dsel, :, :], func=AF.Sign)
                t[("wguT_sgn", hc, m)] = sAC.tok()
            b.sec("scalar", sg)

        def pe1(e, hc=hc, dsel=dsel):
            if hc == 0:
                wt(e, t["xq_done"])
            wt(e, t[("wguT_sgn", hc, 1)])
            wt(e, t["pg_free"][dsel])
            pg = b.pp[0 + dsel]
            pu = b.pp[2 + dsel]
            for fc in range(FC):
                for half in range(2):
                    e.matmul(out=pg[:, half * 512:(half + 1) * 512],
                             lhsT=t["wgT"][:, dsel, fc, :],
                             rhs=t["xqfm"][:, half * 4:half * 4 + 4, fc, :],
                             start=(fc == 0), stop=(fc == FC - 1))
            for fc in range(FC):
                for half in range(2):
                    mm = e.matmul(out=pu[:, half * 512:(half + 1) * 512],
                                  lhsT=t["wuT"][:, dsel, fc, :],
                                  rhs=t["xqfm"][:, half * 4:half * 4 + 4,
                                               fc, :],
                                  start=(fc == 0), stop=(fc == FC - 1))
                    if fc == FC - 1 and half == 1:
                        t[("mm12_done", hc)] = sPE.inc(mm)
        b.sec("tensor", pe1)

        def ac3(e, hc=hc, dsel=dsel):
            wt(e, t[("mm12_done", hc)])
            wt(e, t["pg_free"][1 - dsel])
            e.activation(out=t["sgb"][:], in_=b.pp[0 + dsel][:],
                             func=AF.Sigmoid)
            t["sgb_ready"] = sAC.tok()
        b.sec("scalar", ac3)

        def ve4(e, hc=hc, dsel=dsel):
            wt(e, t["sgb_ready"])
            wt(e, t["pout_free"][dsel])
            e.tensor_tensor(out=t["silu"][:], in0=b.pp[0 + dsel][:],
                            in1=t["sgb"][:], op=OP.mult)
            e.tensor_tensor(out=t["pout"][:, dsel, :],
                                          in0=t["silu"][:],
                                          in1=b.pp[2 + dsel][:], op=OP.mult)
            tok = sDV.tok()
            t["pg_free"][dsel] = tok
            t["pout_ready"] = tok
        b.sec("vector", ve4)

        def stp(e, hc=hc, dsel=dsel):
            wt(e, t["pout_ready"])
            t["pout_free"][dsel] = s[f"sPo{dsel}"].inc(
                e.dma_start(out=p_dram[hc], in_=t["pout"][:, dsel, :]), 16)
        b.sec("scalar", stp)
    t["ffn12_done"] = sDV.tok()
    p_stored = [t["pout_free"][0], t["pout_free"][1]]

    t["pso_free"] = {}
    t["pin_ready"] = [None] * 4
    for ogp in range(2):
        for tch in range(2):
            for hc in range(HC):
                sel = hc % 2
                slot = hc % 4
                sP = s[f"sPin{slot}"]

                def ldp(e, ogp=ogp, tch=tch, hc=hc, slot=slot, sP=sP):
                    if ogp == 0 and tch == 0 and hc == 0:
                        wt(e, p_stored[0])
                        wt(e, p_stored[1])
                    wt(e, t.get(("mm3_hc", ogp, tch, hc - 4)))
                    t["pin_ready"][slot] = sP.inc(
                        e.dma_start(out=t["pin"][:, slot, :], in_=p_dram[hc]),
                        16)
                b.sec("scalar", ldp)

                def trd(e, ogp=ogp, tch=tch, hc=hc, sel=sel):
                    if ogp == 0 and tch == 0 and hc == 0:
                        wt(e, t["wd_scratch_done"])
                    wt(e, t.get(("mm3_hc", ogp, tch, hc - 2)))
                    t["wdT_tr"] = tr_start(
                        b, e, t["wdT"][:, sel, :, :],
                        wd_sgn[ogp * 1024:(ogp + 1) * 1024,
                               hc * 128:(hc + 1) * 128])
                b.sec("sync", trd)

                def pe3(e, ogp=ogp, tch=tch, hc=hc, sel=sel, slot=slot):
                    wt(e, t["wdT_tr"])
                    wt(e, t["pin_ready"][slot])
                    if hc == 0:
                        for k in range(8):
                            wt(e, t["pso_free"].get(k))
                    for oi in range(2):
                        for q in range(4):
                            tc = tch * 4 + q
                            mm = e.matmul(
                                out=b.bank[oi * 4 + q],
                                lhsT=t["pin"][:, slot,
                                              tc * 128:(tc + 1) * 128],
                                rhs=t["wdT"][:, sel, 0,
                                             oi * 512:(oi + 1) * 512],
                                start=(hc == 0), stop=(hc == HC - 1))
                            if oi == 1 and q == 3:
                                t[("mm3_hc", ogp, tch, hc)] = sPE.inc(mm)
                b.sec("tensor", pe3)
            mm3_last = t[("mm3_hc", ogp, tch, HC - 1)]

            for oi in range(2):
                for q in range(4):
                    tc = tch * 4 + q
                    og = ogp * 2 + oi
                    k = oi * 4 + q
                    sl2 = k % 2

                    def ac4(e, k=k, sl2=sl2, mm3_last=mm3_last):
                        wt(e, mm3_last)
                        wt(e, t.get(("gf_evac", sl2)))
                        e.activation(out=t["sgb2"][:, sl2, :],
                                         in_=b.bank[k], func=AF.Sigmoid)
                        t["sg2_ready"] = sAC.tok()
                    b.sec("scalar", ac4)

                    def ve5(e, og=og, tc=tc, k=k, sl2=sl2):
                        wt(e, t["sg2_ready"])
                        e.tensor_tensor(
                            out=t["gf"][:, tc, og * 512:(og + 1) * 512],
                            in0=b.bank[k], in1=t["sgb2"][:, sl2, :],
                            op=OP.mult)
                        tok = sDV.tok()
                        t["pso_free"][k] = tok
                        t[("gf_evac", sl2)] = tok
                    b.sec("vector", ve5)
    t["ffn_done"] = sDV.tok()


def emit_new_mem_aug(b, mem, nrm, dsum):
    s, t = b.s, b.t
    emit_nrm_load(b, nrm)
    t["mtmp_free"] = [None, None]
    for i in range(96):
        hh, fcc = divmod(i, 6)
        sel = 0
        sM = s[f"sMem{i % 2}"]

        def sc(e, i=i, hh=hh, fcc=fcc, sel=sel, sM=sM):
            if i == 0:
                wt(e, t["dsum_done"])
            wt(e, t["mtmp_free"][sel])
            e.dma_start(out=t["mtmp"][:, sel, :],
                        in_=mem[hh, fcc * 128:(fcc + 1) * 128, :]).then_inc(
                sM.h, 16)
            sM.v += 16
            t["mtmp_ready"] = sM.inc(
                e.dma_start(out=t["dtile"][:, sel, :],
                            in_=dsum[i * 128:(i + 1) * 128, :]), 16)
        b.sec("scalar", sc)

        def ve(e, i=i, sel=sel):
            wt(e, t["mtmp_ready"])
            if i == 0:
                wt(e, t["nrmT_ready"])
            e.tensor_tensor(out=t["mem_aug"][:, i, 0:128],
                            in0=t["mtmp"][:, sel, :],
                            in1=t["dtile"][:, sel, 0:128], op=OP.add)
            e.tensor_tensor(out=t["mem_aug"][:, i, 128:129],
                                in0=t["nrmT"][:, 0, i:i + 1],
                                in1=t["dtile"][:, sel, 128:129], op=OP.add)
            t["mtmp_free"][sel] = s["sDV"].tok()
        b.sec("vector", ve)
    t["memaug_done"] = s["sDV"].tok()


def emit_phase_b(b, x, out, Wq, Wg):
    s, t = b.s, b.t
    sPE, sDV, sAC, sTr = s["sPE"], s["sDV"], s["sAC"], s["sTr"]
    t["outsb_free"] = None
    for grp in range(2):
        for q in range(4):
            tc = grp * 4 + q
            sel = tc % 2
            sX = s[f"sXt{sel}"]

            def ld(e, tc=tc, sel=sel, sX=sX):
                wt(e, t["xt_free"][sel])
                t["xt_ready"][sel] = sX.inc(
                    e.dma_start(out=t["xt"][:, sel, :],
                                in_=x[tc * 128:(tc + 1) * 128, :]), 16)
            b.sec("scalar", ld)

            def ve(e, grp=grp, q=q, tc=tc, sel=sel):
                wt(e, t["xt_ready"][sel])
                wt(e, t.get("xnbB_free"))
                e.tensor_scalar(
                    out=t["xnbB"][:], in0=t["xt"][:, sel, :],
                    scalar1=t["stats_m"][:, tc:tc + 1],
                    scalar2=t["stats_r"][:, tc:tc + 1],
                    op0=OP.subtract, op1=OP.mult)
                tok = sDV.tok()
                t["xnbB_ready"] = tok
                t["xt_free"][sel] = tok
            b.sec("vector", ve)

            def tr(e, grp=grp, q=q):
                wt(e, t["xnbB_ready"])
                wt(e, t.get("xnfmB_guard"))
                tok = tr_start(b, e, t["xnfmB"][:, q, :, :], t["xnbB"][:])
                t["xnbB_free"] = tok
                t["xnfmB_ready"] = tok
            b.sec("sync", tr)

        for oc in range(4):
            for m, W, dstn in [(0, Wq, "WqTs"), (1, Wg, "WgTs")]:
                for j in range(4):
                    emit_weight_cast_transpose(
                        b, W, oc * 4 + j, t[dstn][:, j, :, :],
                        t.get(("slab_free", dstn)), wnat_name="wnatB",
                        nslots=1)
                    t[("slab_tr", dstn)] = t["last_wtr"]

            for q in range(4):
                tc = grp * 4 + q
                pqb = b.bank[0 + q % 2]
                pgb = b.bank[2 + q % 2]

                def peb(e, grp=grp, oc=oc, q=q, tc=tc, pqb=pqb, pgb=pgb):
                    if q == 0:
                        wt(e, t[("slab_tr", "WgTs")])
                        if grp == 0 and oc == 0:
                            wt(e, t["memaug_done"])
                            wt(e, t["bias_done"])
                    if q == 3:
                        wt(e, t["xnfmB_ready"])
                    wt(e, b.bank_free[0 + q % 2])
                    wt(e, b.bank_free[2 + q % 2])
                    for fc in range(FC):
                        lhs = t["xnfmB"][:, q, fc, :]
                        e.matmul(out=pqb, lhsT=lhs,
                                 rhs=t["WqTs"][:, :, fc, :],
                                 start=(fc == 0), stop=False)
                        e.matmul(out=pgb, lhsT=lhs,
                                 rhs=t["WgTs"][:, :, fc, :],
                                 start=(fc == 0), stop=False)
                    e.matmul(out=pqb, lhsT=t["ones"][:, 0:128],
                             rhs=t["bqr"][0:1, oc * 512:(oc + 1) * 512],
                             start=False, stop=True)
                    t["projB_done"] = sPE.inc(
                        e.matmul(out=pgb, lhsT=t["ones"][:, 0:128],
                                 rhs=t["bgr"][0:1, oc * 512:(oc + 1) * 512],
                                 start=False, stop=True))
                    if q == 3:
                        t[("slab_free", "WqTs")] = t["projB_done"]
                        t[("slab_free", "WgTs")] = t["projB_done"]
                b.sec("tensor", peb)

                def veb(e, oc=oc, q=q, pqb=pqb):
                    wt(e, t["projB_done"])
                    wt(e, t.get("qf4_guard"))
                    e.tensor_copy(
                        out=t["qf4"][:, q, oc * 512:(oc + 1) * 512], in_=pqb)
                    tok = sDV.tok()
                    b.bank_free[0 + q % 2] = tok
                    t["qf4_ready"] = tok
                b.sec("vector", veb)

                def acb(e, oc=oc, q=q, pgb=pgb):
                    wt(e, t["projB_done"])
                    wt(e, t.get("gate4_guard"))
                    e.activation(
                        out=t["gate4"][:, q, oc * 512:(oc + 1) * 512],
                        in_=pgb, func=AF.Sigmoid)
                    tok = sAC.tok()
                    b.bank_free[2 + q % 2] = tok
                    t["gate4_ready"] = tok
                b.sec("scalar", acb)

        for q in range(4):
            tc = grp * 4 + q
            sel = tc % 2
            sX = s[f"sXt{sel}"]

            def ld2(e, tc=tc, sel=sel, sX=sX):
                wt(e, t["xt_free"][sel])
                t["xt_ready"][sel] = sX.inc(
                    e.dma_start(out=t["xt"][:, sel, :],
                                in_=x[tc * 128:(tc + 1) * 128, :]), 16)
            b.sec("scalar", ld2)

            for h in range(H):
                def acr(e, q=q, h=h):
                    if h == 0:
                        wt(e, t["qf4_ready"])
                        wt(e, t["gate4_ready"])
                    wt(e, t.get("mqpre_ready"))
                    qs = t["qf4"][:, q, h * 128:(h + 1) * 128]
                    e.activation(out=t["xp"][:, 0:128], in_=qs, func=AF.Relu)
                    e.activation(out=t["xp"][:, 128:256], in_=qs,
                                     func=AF.Relu, scale=-1.0)
                    t["xp_ready"] = sAC.tok()
                b.sec("scalar", acr)

                def ver(e, q=q, h=h):
                    wt(e, t["xp_ready"])
                    xp, mp = t["xp"], t["mq_pre"]
                    for j in range(3):
                        sh = j + 1
                        e.tensor_tensor(out=mp[:, j * 256 + sh:j * 256 + 256],
                                        in0=xp[:, sh:256],
                                        in1=xp[:, 0:256 - sh], op=OP.mult)
                        e.tensor_tensor(out=mp[:, j * 256:j * 256 + sh],
                                        in0=xp[:, 0:sh],
                                        in1=xp[:, 256 - sh:256], op=OP.mult)
                    t["mqpre_ready"] = sDV.tok()
                b.sec("vector", ver)

                def acs(e, q=q, h=h):
                    wt(e, t["mqpre_ready"])
                    e.activation(out=t["junka"][:, 0:F], in_=t["mq_pre"][:],
                                 func=AF.Square, accum_out=t["ssq"][:])
                    e.activation(out=t["ncol"][:], in_=t["ssq"][:],
                                     func=AF.Sqrt)
                    t["n2_ready"] = sAC.tok()
                b.sec("scalar", acs)

                def vemq(e, q=q, h=h):
                    wt(e, t["n2_ready"])
                    wt(e, t.get("mq_tm_free"))
                    e.tensor_scalar(out=t["nmax"][:], in0=t["ncol"][:],
                                    scalar1=EPS_L2, scalar2=None, op0=OP.max)
                    e.reciprocal(out=t["inv"][:], in_=t["nmax"][:])
                    e.tensor_scalar(out=t["mq_tm"][:], in0=t["mq_pre"][:],
                                        scalar1=t["inv"][:], scalar2=None,
                                        op0=OP.mult)
                    t["mq_ready"] = sDV.tok()
                b.sec("vector", vemq)

                def trmq(e, q=q, h=h):
                    wt(e, t["mq_ready"])
                    wt(e, t.get("mqfm_free"))
                    tok = tr_start(b, e, t["mqfm"][:], t["mq_tm"][:])
                    t["mq_tm_free"] = tok
                    t["mqfm_ready"] = tok
                b.sec("sync", trmq)

                pn = b.bank[4 + h % 2]

                def pen(e, q=q, h=h, pn=pn):
                    wt(e, t["mqfm_ready"])
                    wt(e, b.bank_free[4 + h % 2])
                    for fcc in range(6):
                        mm = e.matmul(out=pn[:, 0:129],
                                      lhsT=t["mqfm"][:, fcc, :],
                                      rhs=t["mem_aug"][:, h * 6 + fcc, :],
                                      start=(fcc == 0), stop=(fcc == 5))
                        if fcc == 5:
                            t["pnB_done"] = sPE.inc(mm)
                    t["mqfm_free"] = t["pnB_done"]
                b.sec("tensor", pen)

                def vout(e, q=q, h=h, tc=tc, sel=sel, pn=pn):
                    wt(e, t["pnB_done"])
                    if h == 0:
                        wt(e, t["outsb_free"])
                        wt(e, t["xt_ready"][sel])
                    e.tensor_scalar(out=t["dcol"][:], in0=pn[:, 128:129],
                                    scalar1=EPS_A, scalar2=None, op0=OP.add)
                    e.reciprocal(out=t["rden"][:], in_=t["dcol"][:])
                    e.tensor_scalar(out=t["mhb"][:], in0=pn[:, 0:128],
                                    scalar1=t["rden"][:], scalar2=None,
                                    op0=OP.mult)
                    e.tensor_scalar(out=t["xnh"][:],
                                    in0=t["xt"][:, sel, h * 128:(h + 1) * 128],
                                    scalar1=t["stats_m"][:, tc:tc + 1],
                                    scalar2=t["stats_r"][:, tc:tc + 1],
                                    op0=OP.subtract, op1=OP.mult)
                    e.tensor_tensor(out=t["sh"][:], in0=t["mhb"][:],
                                    in1=t["gate4"][:, q,
                                                   h * 128:(h + 1) * 128],
                                    op=OP.mult)
                    e.tensor_tensor(out=t["sh"][:], in0=t["sh"][:],
                                    in1=t["xnh"][:], op=OP.add)
                    e.tensor_tensor(out=t["oh"][:], in0=t["sh"][:],
                                    in1=t["gf"][:, tc, h * 128:(h + 1) * 128],
                                    op=OP.mult)
                    e.tensor_tensor(
                        out=t["outsb"][:, h * 128:(h + 1) * 128],
                        in0=t["oh"][:],
                        in1=t["xt"][:, sel, h * 128:(h + 1) * 128],
                        op=OP.add)
                    tok = sDV.tok()
                    b.bank_free[4 + h % 2] = tok
                    t["outsb_ready"] = tok
                    if h == H - 1:
                        t["xt_free"][sel] = tok
                b.sec("vector", vout)

            def sto(e, q=q, tc=tc):
                wt(e, t["outsb_ready"])
                t["outsb_free"] = s["sOut"].inc(
                    e.dma_start(out=out[tc * 128:(tc + 1) * 128, :],
                                in_=t["outsb"][:]), 16)
            b.sec("scalar", sto)

        t["qf4_guard"] = sAC.tok()
        t["gate4_guard"] = sDV.tok()
        t["xnfmB_guard"] = sPE.tok()



_CACHE = {}


def kernel(**inputs):
    X = np.ascontiguousarray(
        np.asarray(inputs["L0_attn_hidden_states"], np.float32)
        .reshape(8192, D))
    g = lambda n: np.ascontiguousarray(np.asarray(inputs[n], np.float32))
    Wq, bq = g("L0_attn_Wq"), g("L0_attn_bq")
    Wk, bk = g("L0_attn_Wk"), g("L0_attn_bk")
    Wv, bv = g("L0_attn_Wv"), g("L0_attn_bv")
    Wg, bg = g("L0_attn_Wg"), g("L0_attn_bg")
    Wmb, bmb = g("L0_attn_Wmb"), g("L0_attn_bmb")
    mem, nrm = g("L0_attn_memory"), g("L0_attn_norm")
    Wgf, Wuf, Wdf = g("L0_ffn_W_g"), g("L0_ffn_W_u"), g("L0_ffn_W_d")

    if "l1" not in _CACHE:
        _CACHE["l1"] = build_launch1()
    nc1 = _CACHE["l1"]
    maps1 = [{"x": X[c * NTOK:(c + 1) * NTOK], "Wk": Wk, "bk": bk, "Wv": Wv,
              "bv": bv, "Wmb": Wmb, "bmb": bmb, "mem": mem, "nrm": nrm}
             for c in range(8)]
    res1 = run_bass_kernel_spmd(nc1, maps1, core_ids=list(range(8)))
    deltas = [np.asarray(res1.results[c]["delta"], np.float32)
              for c in range(8)]

    if "l2" not in _CACHE:
        _CACHE["l2"] = build_launch2()
    nc2 = _CACHE["l2"]
    maps2 = []
    for c in range(8):
        m = {"x": X[c * NTOK:(c + 1) * NTOK], "Wq": Wq, "bq": bq, "Wg": Wg,
             "bg": bg, "mem": mem, "nrm": nrm, "Wgf": Wgf, "Wuf": Wuf,
             "Wdf": Wdf}
        for i in range(8):
            m[f"d{i}"] = deltas[i]
        maps2.append(m)
    res2 = run_bass_kernel_spmd(nc2, maps2, core_ids=list(range(8)))
    out = np.concatenate([np.asarray(res2.results[c]["out"], np.float32)
                          for c in range(8)], axis=0)
    return out.reshape(1, 8192, D)
